# version 33
# baseline (speedup 1.0000x reference)
import math
from contextlib import ExitStack

import numpy as np

import concourse.bass as bass
import concourse.mybir as mybir
from concourse.bass_utils import run_bass_kernel_spmd

F32 = mybir.dt.float32
BF16 = mybir.dt.bfloat16
AF = mybir.ActivationFunctionType
ALU = mybir.AluOpType
AX = mybir.AxisListType

D = 2048
NDC = 16
SEQ = 2048
NTC = 16
QOWN = 1024
NH = 16
HD = 128
NE = 8
SCALE = 1.0 / math.sqrt(HD)
EPS = 1e-5

_ENG_SECTIONS = {"pe": "tensor", "act": "scalar", "dve": "vector",
                 "sync": "sync", "pool": "gpsimd"}


class Prog:

    def __init__(self, nc, st):
        self.nc = nc
        self.st = st
        self.items = {e: [] for e in _ENG_SECTIONS}
        self.cnt = {}
        self.sem = {}
        for e in _ENG_SECTIONS:
            self._sem(e)
        self.water = {e: {} for e in _ENG_SECTIONS}
        self.lastw = {}
        self.readers = {}
        self.targets = set()
        self.dma_sems = set()

    def _sem(self, name):
        if name not in self.sem:
            self.sem[name] = self.st.enter_context(
                self.nc.semaphore("sem_" + name))
            self.cnt[name] = 0
        return name

    def emit(self, eng, fn, reads=(), writes=(), force_event=False, dsem=None):
        isq = eng in ("sync", "pool")
        if isq and dsem == "gq":
            sname = self._sem("gq")
            isq = False
            force_event = True
        elif isq:
            sname = self._sem(dsem or ("d_" + (writes[0] if writes else eng)))
            self.dma_sems.add(sname)
        else:
            sname = eng
        waits = {}

        def need(evmap):
            for e2, c2 in (evmap or {}).items():
                if e2 == "pe" and eng == "pe":
                    continue
                if self.water[eng].get(e2, 0) >= c2:
                    continue
                waits[e2] = max(waits.get(e2, 0), c2)

        for k in reads:
            need(self.lastw.get(k))
        for k in writes:
            need(self.lastw.get(k))
            need(self.readers.get(k))
        for e2 in sorted(waits):
            c2 = waits[e2]
            self.items[eng].append(("w", e2, c2))
            self.water[eng][e2] = c2
            self.targets.add((e2, c2))
        self.cnt[sname] += 1
        c = self.cnt[sname]
        self.items[eng].append(("i", fn, sname, c))
        if force_event or isq:
            self.targets.add((sname, c))
        for k in reads:
            self.readers.setdefault(k, {})[sname] = c
        for k in writes:
            self.lastw[k] = {sname: c}
            self.readers[k] = {}
        return (sname, c)

    def barrier_wait(self, eng, ev_sem, ev_cnt):
        if self.water[eng].get(ev_sem, 0) < ev_cnt:
            self.items[eng].append(("w", ev_sem, ev_cnt))
            self.water[eng][ev_sem] = ev_cnt
            self.targets.add((ev_sem, ev_cnt))

    def finalize(self, block):
        prefix = {}
        for eng in _ENG_SECTIONS:
            if eng in ("sync", "pool"):
                continue
            n = 0
            idx = 0
            pre = [0]
            for lst_eng, items in self.items.items():
                pass
            for it in self.items[eng]:
                if it[0] == "i" and it[2] == eng:
                    idx += 1
                    if (eng, idx) in self.targets:
                        n += 1
                    pre.append(n)
            prefix[eng] = pre

        def waitval(sname, c):
            if sname in self.dma_sems:
                return 16 * c
            if sname in prefix:
                return prefix[sname][c]
            return c

        for eng, section in _ENG_SECTIONS.items():
            items = self.items[eng]
            sem = self.sem
            tgt = self.targets

            def body(E, items=items, eng=eng):
                for it in items:
                    if it[0] == "w":
                        E.wait_ge(sem[it[1]], waitval(it[1], it[2]))
                    else:
                        instr = it[1](E)
                        sname = it[2]
                        if (sname, it[3]) in tgt:
                            instr.then_inc(
                                sem[sname],
                                16 if sname in self.dma_sems else 1)

            getattr(block, section)(body)


def build_program(debug=False):
    nc = bass.Bass("TRN2", target_bir_lowering=False, debug=False)

    xtok = nc.dram_tensor("xtok", [SEQ, D], F32, kind="ExternalInput")
    xT = nc.dram_tensor("xT", [D, QOWN], F32, kind="ExternalInput")
    wq4 = nc.dram_tensor("wq4", [NH, 128, NDC, 128], F32, kind="ExternalInput")
    wk4 = nc.dram_tensor("wk4", [NH, 128, NDC, 128], F32, kind="ExternalInput")
    wv4 = nc.dram_tensor("wv4", [NH, 128, NDC, 128], F32, kind="ExternalInput")
    wo4 = nc.dram_tensor("wo4", [NDC, 128, NDC, 128], F32, kind="ExternalInput")
    wb4 = nc.dram_tensor("wb4", [NDC, 128, NDC, 128], F32, kind="ExternalInput")
    bbd = nc.dram_tensor("bbd", [128, NDC], F32, kind="ExternalInput")
    rwd = nc.dram_tensor("rwd", [128, NDC, NE], F32, kind="ExternalInput")
    atd = nc.dram_tensor("atd", [128, NDC, 128], F32, kind="ExternalInput")
    wbld = nc.dram_tensor("wbld", [128, D], F32, kind="ExternalInput")
    eb_d = nc.dram_tensor("eb_d", [NE, 128], F32, kind="ExternalInput")
    id_d = nc.dram_tensor("id_d", [128, 128], F32, kind="ExternalInput")
    cmb_d = nc.dram_tensor("cmb_d", [128, 1], F32, kind="ExternalInput")
    outT = nc.dram_tensor("outT", [D, QOWN], F32, kind="ExternalOutput")
    dbg = {}
    if debug:
        dbg["xnT"] = nc.dram_tensor("d_xnT", [128, NDC * SEQ], F32, kind="ExternalOutput")
        dbg["kt"] = nc.dram_tensor("d_kt", [128, SEQ], F32, kind="ExternalOutput")
        dbg["qt"] = nc.dram_tensor("d_qt", [128, QOWN], F32, kind="ExternalOutput")
        dbg["vh"] = nc.dram_tensor("d_vh", [128, SEQ], F32, kind="ExternalOutput")
        dbg["ct"] = nc.dram_tensor("d_ct", [128, NH * QOWN], F32, kind="ExternalOutput")
        dbg["logits"] = nc.dram_tensor("d_logits", [128, 64], F32, kind="ExternalOutput")
        dbg["hm"] = nc.dram_tensor("d_hm", [128, QOWN], F32, kind="ExternalOutput")
        dbg["mer"] = nc.dram_tensor("d_mer", [128, QOWN], F32, kind="ExternalOutput")

    with ExitStack() as st:
        def sb(name, shape, dtype):
            return st.enter_context(nc.sbuf_tensor(name, shape, dtype))

        def psum(name, shape, dtype=F32):
            return st.enter_context(nc.psum_tensor(name, shape, dtype))

        xnT = sb("xnT", [128, NDC * SEQ], BF16)
        ctall = sb("ctall", [128, NH * QOWN], BF16)
        vh = sb("vh", [128, 2 * SEQ], BF16)
        vt = sb("vt", [128, SEQ], BF16)
        kt = sb("kt", [128, 2 * SEQ], BF16)
        qt = sb("qt", [128, 2 * QOWN], BF16)
        pt = sb("pt", [128, NTC * 512], BF16)
        wqs = sb("wqs", [128, D], BF16)
        wks = sb("wks", [128, D], BF16)
        wvs = sb("wvs", [128, D], BF16)
        xst = sb("xst", [128, 2 * D], F32)
        xns = sb("xns", [128, 2 * D], BF16)
        outb = sb("outb", [128, 2 * QOWN], F32)
        ats = sb("ats", [128, D], BF16)
        wbls = sb("wbls", [128, D], BF16)
        mer = sb("mer", [128, QOWN], BF16)
        hm = sb("hm", [128, QOWN], BF16)
        maskT = sb("maskT", [8, QOWN], BF16)
        logits = sb("logits", [128, 64], F32)
        mask8 = sb("mask8", [128, 8], BF16)
        bc = sb("bc", [128, 512], F32)
        lnl = sb("lnl", [1, 512], F32)
        rlb = sb("rlb", [1, 512], BF16)
        st6 = sb("st6", [128, 48], F32)
        st2 = sb("st2", [128, 32], F32)
        sqv = sb("sqv", [128, 2], F32)
        rcol = sb("rcol", [128, 16], F32)
        epsc = sb("epsc", [128, 1], F32)
        twoc = sb("twoc", [128, 1], F32)
        mxc = sb("mxc", [128, 1], F32)
        idbf = sb("idbf", [128, 128], BF16)
        ebfs = sb("ebfs", [8, 128], BF16)
        onesc = sb("onesc", [128, 1], BF16)
        cmbs = sb("cmbs", [128, 1], BF16)
        lpart = sb("lpart", [128, 512], BF16)
        onesr = sb("onesr", [1, 128], BF16)
        bbs = sb("bbs", [128, NDC], F32)
        rws = sb("rws", [128, NDC * NE], BF16)

        psA = psum("psA", [128, 2048], F32)
        psB = psum("psB", [128, 1024], F32)
        psC = psum("psC", [128, 512], F32)
        psE = psum("psE", [128, 512], F32)

        def psa(i):
            return psA[:, i * 512:(i + 1) * 512]

        def psabf(i):
            return psA[:, i * 512:(i + 1) * 512].bitcast(BF16)

        def psbbf(i):
            return psB[:, i * 512:(i + 1) * 512].bitcast(BF16)

        p = Prog(nc, st)
        xnT3 = xnT.rearrange("p (d t) -> p d t", t=SEQ)

        def xnt(dc, lo, hi):
            return xnT[:, dc * SEQ + lo: dc * SEQ + hi]

        XNALL = [f"xnT{t}" for t in range(NTC)]

        p.emit("dve", lambda E: E.memset(psE[:], 0.0), writes=["psE"])
        p.emit("dve", lambda E: E.memset(epsc[:], EPS), writes=["epsc"])
        p.emit("dve", lambda E: E.memset(twoc[:], 2.0), writes=["twoc"])
        p.emit("dve", lambda E: E.memset(onesc[:], 1.0), writes=["onesc"])
        p.emit("dve", lambda E: E.memset(onesr[:], 1.0), writes=["onesr"])
        p.emit("pool", lambda E: E.dma_start(idbf[:], id_d[:]), writes=["idbf"])
        p.emit("pool", lambda E: E.dma_start(cmbs[:], cmb_d[:]), writes=["cmbs"])
        p.emit("pool", lambda E: E.dma_start(ebfs[:], eb_d[:]), writes=["ebfs"])
        p.emit("pool", lambda E: E.dma_start(
            rws[:], rwd[:].rearrange("p a b -> p (a b)")), writes=["rws"])
        p.emit("pool", lambda E: E.dma_start(
            ats[:], atd[:].rearrange("p a b -> p (a b)")), writes=["ats"])
        p.emit("pool", lambda E: E.dma_start(wbls[:], wbld[:]), writes=["wbls"])
        p.emit("sync", lambda E: E.dma_start(bbs[:], bbd[:]), writes=["bbs"])

        ptf = pt[:].bitcast(F32)

        xslots = [(xst[:, 0:D], "xs0", []),
                  (xst[:, D:2 * D], "xs1", []),
                  (ptf[:, 0:D], "xs2", [f"pt{i}" for i in range(4)]),
                  (ptf[:, D:2 * D], "xs3", [f"pt{i}" for i in range(4, 8)])]

        def ln_stats(tc):
            if tc >= NTC:
                return
            sl = tc % 2
            xv, xk, ptk = xslots[tc % 4]
            p.emit("sync", lambda E, tc=tc, xv=xv: E.dma_start(
                xv, xtok[tc * 128:(tc + 1) * 128, :]), writes=[xk])
            s6 = st6[:, sl * 24:(sl + 1) * 24]
            for i in range(4):
                p.emit("dve", lambda E, xv=xv, i=i, s6=s6: E.bn_stats(
                    s6[:, i * 6:(i + 1) * 6], xv[:, i * 512:(i + 1) * 512]),
                    reads=[xk], writes=[f"st6{sl}"])
            s2 = st2[:, tc * 2:(tc + 1) * 2]
            p.emit("dve", lambda E, s6=s6, s2=s2: E.bn_aggr(s2, s6),
                   reads=[f"st6{sl}"], writes=[f"st2_{tc}"])
            p.emit("act", lambda E, s2=s2, sl=sl: E.activation(
                sqv[:, sl:sl + 1], s2[:, 1:2], AF.Ln, bias=epsc[:, 0:1],
                scale=1.0), reads=[f"st2_{tc}", "epsc"], writes=[f"sqv{sl}"])
            p.emit("act", lambda E, sl=sl, tc=tc: E.activation(
                rcol[:, tc:tc + 1], sqv[:, sl:sl + 1], AF.Exp, scale=-0.5),
                reads=[f"sqv{sl}"], writes=[f"rcol{tc}"])

        def ln_norm(tc):
            sl = tc % 2
            nxv, xk, ptk = xslots[tc % 4]
            xnv = xns[:, sl * D:(sl + 1) * D]
            xnk = f"xns{sl}"
            s2 = st2[:, tc * 2:(tc + 1) * 2]
            p.emit("dve", lambda E, nxv=nxv, xnv=xnv, s2=s2, tc=tc: E.tensor_scalar(
                xnv, nxv, s2[:, 0:1], rcol[:, tc:tc + 1],
                ALU.subtract, ALU.mult),
                reads=[xk, f"st2_{tc}", f"rcol{tc}"] + ptk, writes=[xnk])
            b0, b1 = (0, 1) if tc % 2 == 0 else (2, 3)
            for half, b in ((0, b0), (1, b1)):
                for j in range(8):
                    dc = half * 8 + j
                    p.emit("pe", lambda E, b=b, j=j, dc=dc, xnv=xnv: E.transpose(
                        psabf(b)[:, j * 128:(j + 1) * 128],
                        xnv[:, dc * 128:(dc + 1) * 128], idbf[:]),
                        reads=[xnk, "idbf"], writes=[f"psA{b}"])
                p.emit("act", lambda E, b=b, half=half, tc=tc: E.copy(
                    xnT3[:, half * 8:(half + 1) * 8, tc * 128:(tc + 1) * 128],
                    psabf(b).rearrange("p (a b) -> p a b", b=128)),
                    reads=[f"psA{b}"], writes=[f"xnT{tc}"])
            rtc = tc - 1
            if 0 <= rtc < 8:
                router(rtc)
            if tc >= 8:
                mask_chain(tc - 8)

        def mask_chain(tc):
            lsl = logits[:, tc * 8:(tc + 1) * 8]
            pscbf = psC[:].bitcast(BF16)
            p.emit("dve", lambda E, lsl=lsl: E.reduce_max(mxc[:], lsl, axis=AX.X),
                   reads=[f"logits{tc}"], writes=["mxc"])
            p.emit("dve", lambda E, lsl=lsl: E.tensor_scalar(
                mask8[:], lsl, mxc[:, 0:1], None, ALU.is_ge),
                reads=[f"logits{tc}", "mxc"], writes=["mask8"])
            p.emit("pe", lambda E, pscbf=pscbf: E.transpose(
                pscbf[:, 0:128][0:8, :], mask8[:], idbf[:]),
                reads=["mask8", "idbf"], writes=["psC"])
            p.emit("act", lambda E, tc=tc, pscbf=pscbf: E.copy(
                maskT[:, tc * 128:(tc + 1) * 128], pscbf[0:8, 0:128]),
                reads=["psC"], writes=["maskT"])

        def router(tc):
            for dc in range(NDC):
                p.emit("pe", lambda E, tc=tc, dc=dc: E.matmul(
                    psE[:, 0:8], xnt(dc, tc * 128, (tc + 1) * 128),
                    rws[:, dc * 8:(dc + 1) * 8],
                    start=(dc == 0), stop=(dc == NDC - 1)),
                    reads=[f"xnT{tc}", "rws"], writes=["psE"])
            p.emit("act", lambda E, tc=tc: E.copy(
                logits[:, tc * 8:(tc + 1) * 8], psE[:, 0:8]),
                reads=["psE"], writes=[f"logits{tc}"])

        ln_stats(0)
        ln_stats(1)
        for tc in range(NTC):
            ln_norm(tc)
            ln_stats(tc + 2)

        def b2():
            for nb in range(2):
                p.emit("pe", lambda E, nb=nb: E.matmul(
                    psB[:, nb * 512:(nb + 1) * 512], ebfs[:],
                    maskT[:, nb * 512:(nb + 1) * 512], start=True, stop=True),
                    reads=["maskT", "ebfs"], writes=[f"psB{nb}"])
                p.emit("act", lambda E, nb=nb: E.copy(
                    mer[:, nb * 512:(nb + 1) * 512], psB[:, nb * 512:(nb + 1) * 512]),
                    reads=[f"psB{nb}"], writes=["mer"])
            for dc in range(NDC):
                for nb in range(2):
                    p.emit("pe", lambda E, dc=dc, nb=nb: E.matmul(
                        psA[:, 1024 + nb * 512: 1024 + (nb + 1) * 512],
                        ats[:, dc * 128:(dc + 1) * 128],
                        xnt(dc, nb * 512, (nb + 1) * 512),
                        start=(dc == 0), stop=(dc == NDC - 1)),
                        reads=["ats"] + XNALL, writes=[f"psA{2 + nb}"])
            for nb in range(2):
                p.emit("dve", lambda E, nb=nb: E.scalar_tensor_tensor(
                    hm[:, nb * 512:(nb + 1) * 512],
                    psA[:, 1024 + nb * 512:1024 + (nb + 1) * 512],
                    twoc[:, 0:1], mer[:, nb * 512:(nb + 1) * 512],
                    ALU.mult, ALU.mult),
                    reads=[f"psA{2 + nb}", "twoc", "mer"], writes=["hm"])

        def slot_bufs(h):
            sl = h % 2
            return (sl, qt[:, sl * QOWN:(sl + 1) * QOWN], f"qt{sl}",
                    kt[:, sl * SEQ:(sl + 1) * SEQ], f"kt{sl}",
                    vh[:, sl * SEQ:(sl + 1) * SEQ], f"vh{sl}")

        def wdma_qk(h):
            if h >= NH:
                return
            p.emit("pool", lambda E, h=h: E.dma_start(
                wqs[:], wq4[h].rearrange("p a b -> p (a b)")), writes=["wqs"])
            p.emit("pool", lambda E, h=h: E.dma_start(
                wks[:], wk4[h].rearrange("p a b -> p (a b)")), writes=["wks"])

        def wdma_v(h):
            if h >= NH:
                return
            p.emit("pool", lambda E, h=h: E.dma_start(
                wvs[:], wv4[h].rearrange("p a b -> p (a b)")), writes=["wvs"])

        PB = [(psB, 0, "psB0"), (psB, 512, "psB1"), (psA, 1536, "psA3")]

        def gen_proj(h):
            sl, qtv, qtk, ktv, ktk, vhv, vhk = slot_bufs(h)

            def round2(ws, wkey, banks, nbs, dst, dkey):
                for dc in range(NDC):
                    for i, nb in enumerate(nbs):
                        ps_t, col, pk = PB[banks[i]]
                        p.emit("pe", lambda E, dc=dc, nb=nb, ps_t=ps_t,
                               col=col, ws=ws: E.matmul(
                            ps_t[:, col:col + 512],
                            ws[:, dc * 128:(dc + 1) * 128],
                            xnt(dc, nb * 512, (nb + 1) * 512),
                            start=(dc == 0), stop=(dc == NDC - 1)),
                            reads=[wkey] + XNALL, writes=[pk])
                        yield
                for i, nb in enumerate(nbs):
                    ps_t, col, pk = PB[banks[i]]
                    p.emit("dve", lambda E, nb=nb, ps_t=ps_t, col=col,
                           dst=dst: E.tensor_copy(
                        dst[:, nb * 512:(nb + 1) * 512],
                        ps_t[:, col:col + 512]),
                        reads=[pk], writes=[dkey])
                    yield

            yield from round2(wqs, "wqs", [0, 1], [0, 1], qtv, qtk)
            yield from round2(wks, "wks", [2, 0], [0, 1], ktv, ktk)
            yield from round2(wks, "wks", [1, 2], [2, 3], ktv, ktk)
            wdma_qk(h + 1)
            yield from round2(wvs, "wvs", [0, 1], [0, 1], vt, "vt0")
            yield from round2(wvs, "wvs", [2, 0], [2, 3], vt, "vt1")
            wdma_v(h + 1)

        def vtrans(h):
            sl, qtv, qtk, ktv, ktk, vhv, vhk = slot_bufs(h)
            for half in range(2):
                for jj in range(8):
                    kc = half * 8 + jj
                    p.emit("pe", lambda E, half=half, jj=jj, kc=kc: E.transpose(
                        psabf(half)[:, jj * 128:(jj + 1) * 128],
                        vt[:, kc * 128:(kc + 1) * 128], idbf[:]),
                        reads=[f"vt{half}", "idbf"], writes=[f"psA{half}"])
                p.emit("act", lambda E, half=half, vhv=vhv: E.copy(
                    vhv[:, half * 1024:(half + 1) * 1024], psabf(half)),
                    reads=[f"psA{half}"], writes=[vhk])

        def attn_s_weave(h, qb, pgen, per, nbanks, blk=4):
            sl, qtv, qtk, ktv, ktk, vhv, vhk = slot_bufs(h)
            for kc0 in range(0, NTC, blk):
                for kc in range(kc0, min(kc0 + blk, NTC)):
                    bk = kc % nbanks
                    p.emit("pe", lambda E, kc=kc, qb=qb, bk=bk, ktv=ktv, qtv=qtv: E.matmul(
                        psa(bk),
                        ktv[:, kc * 128:(kc + 1) * 128],
                        qtv[:, qb * 512:(qb + 1) * 512],
                        start=True, stop=True),
                        reads=[ktk, qtk], writes=[f"psA{bk}"])
                    p.emit("act", lambda E, kc=kc, bk=bk: E.activation(
                        pt[:, kc * 512:(kc + 1) * 512], psa(bk),
                        AF.Exp, scale=SCALE),
                        reads=[f"psA{bk}"], writes=[f"pt{kc}"])
                if pgen is not None:
                    for _ in range(per * blk):
                        next(pgen, None)

        def attn_av(h, qb):
            sl, qtv, qtk, ktv, ktk, vhv, vhk = slot_bufs(h)
            for r in range(4):
                for j2 in range(4):
                    kc = 4 * r + j2
                    p.emit("pe", lambda E, kc=kc, j2=j2, r=r: E.matmul(
                        psE[32 * j2:32 * j2 + 1, :], onesc[:],
                        pt[:, kc * 512:(kc + 1) * 512],
                        start=(r == 0), stop=(r == 3),
                        tile_position=(0, 32 * j2)),
                        reads=["onesc", f"pt{kc}"], writes=["psE"])
            p.emit("act", lambda E: E.copy(lpart[:], psE[:]),
                   reads=["psE"], writes=["lpart"])
            for kc in range(12):
                p.emit("pe", lambda E, kc=kc, vhv=vhv: E.matmul(
                    psC[:], vhv[:, kc * 128:(kc + 1) * 128],
                    pt[:, kc * 512:(kc + 1) * 512],
                    start=(kc == 0), stop=False),
                    reads=[vhk, f"pt{kc}"], writes=["psC"])
            p.emit("pe", lambda E: E.matmul(
                psE[0:1, :], cmbs[:], lpart[:], start=True, stop=True),
                reads=["cmbs", "lpart"], writes=["psE"])
            p.emit("act", lambda E: E.activation(
                lnl[:], psE[0:1, :], AF.Ln),
                reads=["psE"], writes=["lnl"])
            p.emit("act", lambda E: E.activation(
                rlb[:], lnl[:], AF.Exp, scale=-1.0),
                reads=["lnl"], writes=["rlb"])
            for kc in range(12, NTC):
                p.emit("pe", lambda E, kc=kc, vhv=vhv: E.matmul(
                    psC[:], vhv[:, kc * 128:(kc + 1) * 128],
                    pt[:, kc * 512:(kc + 1) * 512],
                    start=False, stop=(kc == NTC - 1)),
                    reads=[vhk, f"pt{kc}"], writes=["psC"])
            p.emit("pe", lambda E: E.matmul(
                psE[:], onesr[:], rlb[:], start=True, stop=True),
                reads=["onesr", "rlb"], writes=["psE"])
            p.emit("act", lambda E: E.copy(bc[:], psE[:]),
                   reads=["psE"], writes=["bc"])
            p.emit("dve", lambda E, h=h, qb=qb: E.tensor_tensor(
                ctall[:, h * QOWN + qb * 512: h * QOWN + (qb + 1) * 512],
                psC[:], bc[:], ALU.mult),
                reads=["psC", "bc"], writes=["ct"])

        def dpsum(oc):
            if oc % 2 == 0:
                return psB, 0, ["psB0", "psB1"]
            return psA, 1024, ["psA2", "psA3"]

        def gen_dpre(oc):
            ps_t, off, pk = dpsum(oc)
            if oc % 2 == 0:
                wot, wok_keys = wqs, ["wqs"]
                wbv, wbk_keys = wks[:, 0:D], ["wks"]
            else:
                wot, wok_keys = wvs, ["wvs"]
                wbv, wbk_keys = vt[:, 0:D], ["vt0", "vt1"]
            dstate[oc] = (wot, wok_keys)
            p.emit("pool", lambda E, oc=oc, wot=wot: E.dma_start(
                wot[:, 0:D], wo4[oc].rearrange("p a b -> p (a b)")),
                writes=wok_keys)
            p.emit("pool", lambda E, oc=oc, wbv=wbv: E.dma_start(
                wbv, wb4[oc].rearrange("p a b -> p (a b)")), writes=wbk_keys)
            sl = oc % 2
            xv = xst[:, sl * D: sl * D + QOWN]
            p.emit("sync", lambda E, oc=oc, xv=xv: E.dma_start(
                xv, xT[oc * 128:(oc + 1) * 128, :]), writes=[f"xs{sl}"])
            yield
            for dc in range(NDC):
                for nb in range(2):
                    p.emit("pe", lambda E, dc=dc, nb=nb, ps_t=ps_t, off=off,
                           wbv=wbv, wbk_keys=wbk_keys, pk=pk: E.matmul(
                        ps_t[:, off + nb * 512: off + (nb + 1) * 512],
                        wbv[:, dc * 128:(dc + 1) * 128],
                        xnt(dc, nb * 512, (nb + 1) * 512),
                        start=(dc == 0), stop=False),
                        reads=wbk_keys + XNALL, writes=[pk[nb]])
                    yield
            for nb in range(2):
                p.emit("pe", lambda E, oc=oc, nb=nb, ps_t=ps_t, off=off, pk=pk: E.matmul(
                    ps_t[:, off + nb * 512: off + (nb + 1) * 512],
                    wbls[:, oc * 128:(oc + 1) * 128],
                    hm[:, nb * 512:(nb + 1) * 512],
                    start=False, stop=False),
                    reads=["wbls", "hm"], writes=[pk[nb]])
                yield

        def dpost(oc):
            ps_t, off, pk = dpsum(oc)
            wot, wok_keys = dstate[oc]
            sl = oc % 2
            xv = xst[:, sl * D: sl * D + QOWN]
            for dc in range(NDC):
                for nb in range(2):
                    p.emit("pe", lambda E, dc=dc, nb=nb, ps_t=ps_t, off=off, wot=wot: E.matmul(
                        ps_t[:, off + nb * 512: off + (nb + 1) * 512],
                        wot[:, dc * 128:(dc + 1) * 128],
                        ctall[:, dc * QOWN + nb * 512: dc * QOWN + (nb + 1) * 512],
                        start=False, stop=(dc == NDC - 1)),
                        reads=wok_keys + ["ct"], writes=[pk[nb]])
            p.emit("dve", lambda E, oc=oc, sl=sl, xv=xv, ps_t=ps_t, off=off, pk=pk: E.scalar_tensor_tensor(
                outb[:, sl * QOWN:(sl + 1) * QOWN],
                ps_t[:, off: off + 1024],
                bbs[:, oc:oc + 1], xv, ALU.add, ALU.add),
                reads=pk + ["bbs", f"xs{sl}"], writes=[f"outb{sl}"])
            p.emit("sync", lambda E, oc=oc, sl=sl: E.dma_start(
                outT[oc * 128:(oc + 1) * 128, :],
                outb[:, sl * QOWN:(sl + 1) * QOWN]),
                reads=[f"outb{sl}"], writes=[], dsem=f"d_out{sl}")

        dstate = {}

        wdma_qk(0)
        wdma_v(0)
        pg = gen_proj(0)
        for _ in pg:
            pass
        b2()
        vtrans(0)
        for h in range(1, NH):
            pg = gen_proj(h)
            attn_s_weave(h - 1, 0, pg, 5, 3, blk=3)
            attn_av(h - 1, 0)
            attn_s_weave(h - 1, 1, pg, 5, 3, blk=3)
            for _ in pg:
                pass
            vtrans(h)
            attn_av(h - 1, 1)
        dg0 = gen_dpre(0)
        attn_s_weave(NH - 1, 0, dg0, 3, 2, blk=2)
        for _ in dg0:
            pass
        attn_av(NH - 1, 0)
        dg1 = gen_dpre(1)
        attn_s_weave(NH - 1, 1, dg1, 3, 2, blk=2)
        for _ in dg1:
            pass
        attn_av(NH - 1, 1)
        dpost(0)
        dpost(1)
        for oc in range(2, NDC, 2):
            ga, gb = gen_dpre(oc), gen_dpre(oc + 1)
            while True:
                a = next(ga, StopIteration)
                b = next(gb, StopIteration)
                if a is StopIteration and b is StopIteration:
                    break
            dpost(oc)
            dpost(oc + 1)

        if debug:
            for name, src, reads in [
                ("xnT", xnT[:], XNALL), ("kt", kt[:, SEQ:2*SEQ], ["kt1"]),
                ("qt", qt[:, QOWN:2*QOWN], ["qt1"]), ("vh", vh[:, SEQ:2*SEQ], ["vh1"]),
                ("ct", ctall[:], ["ct"]), ("logits", logits[:], [f"logits{t}" for t in range(8)]),
                ("hm", hm[:], ["hm"]), ("mer", mer[:], ["mer"]),
            ]:
                p.emit("pool", lambda E, name=name, src=src: E.dma_start(
                    dbg[name][:], src), reads=reads, writes=["dbg_" + name],
                    dsem="d_dbg")
            p.barrier_wait("sync", "d_dbg", p.cnt.get("d_dbg", 0))

        p.barrier_wait("sync", "d_out0", p.cnt.get("d_out0", 0))
        p.barrier_wait("sync", "d_out1", p.cnt.get("d_out1", 0))

        block = st.enter_context(nc.Block())
        p.finalize(block)
    return nc


_NC_CACHE = {}


def _get_nc(debug=False):
    if debug not in _NC_CACHE:
        _NC_CACHE[debug] = build_program(debug)
    return _NC_CACHE[debug]


def cmb_col():
    c = np.zeros((128, 1), np.float32)
    c[[0, 32, 64, 96], 0] = 1.0
    return c


def _prep_shared(Wq, Wk, Wv, Wo, Wb, bb, rW, A, Bw):
    def pe4(w):
        return np.ascontiguousarray(
            w.reshape(16, 128, 16, 128).transpose(0, 3, 2, 1))

    d = {
        "wq4": pe4(Wq), "wk4": pe4(Wk), "wv4": pe4(Wv),
        "wo4": pe4(Wo), "wb4": pe4(Wb),
        "bbd": np.ascontiguousarray(bb.reshape(16, 128).T),
        "rwd": np.ascontiguousarray(rW.reshape(8, 16, 128).transpose(2, 1, 0)),
        "atd": np.ascontiguousarray(
            A.reshape(128, 2048).reshape(128, 16, 128).transpose(2, 1, 0)),
        "wbld": np.ascontiguousarray(Bw.transpose(0, 2, 1).reshape(128, 2048)),
        "eb_d": np.repeat(np.eye(8, dtype=np.float32), 16, axis=1),
        "id_d": np.eye(128, dtype=np.float32),
        "cmb_d": cmb_col(),
    }
    return d


def kernel(L0_attn_hidden_states, L0_attn_Wq, L0_attn_Wk, L0_attn_Wv,
           L0_attn_Wo, L0_ffn_W_base, L0_ffn_b_base, L0_ffn_router_W,
           L0_ffn_A, L0_ffn_Bw):
    x = np.ascontiguousarray(np.asarray(L0_attn_hidden_states, dtype=np.float32))
    shared = _prep_shared(
        np.asarray(L0_attn_Wq, np.float32), np.asarray(L0_attn_Wk, np.float32),
        np.asarray(L0_attn_Wv, np.float32), np.asarray(L0_attn_Wo, np.float32),
        np.asarray(L0_ffn_W_base, np.float32),
        np.asarray(L0_ffn_b_base, np.float32),
        np.asarray(L0_ffn_router_W, np.float32),
        np.asarray(L0_ffn_A, np.float32), np.asarray(L0_ffn_Bw, np.float32))

    in_maps = []
    for c in range(8):
        s, half = c // 2, c % 2
        r0 = half * QOWN
        own = x[s, r0:r0 + QOWN]
        other = x[s, QOWN - r0:SEQ - r0] if half == 0 else x[s, 0:QOWN]
        m = dict(shared)
        m["xtok"] = np.ascontiguousarray(np.concatenate([own, other], axis=0))
        m["xT"] = np.ascontiguousarray(own.T)
        in_maps.append(m)

    nc = _get_nc(False)
    res = run_bass_kernel_spmd(nc, in_maps, core_ids=list(range(8)))
    out = np.empty((4, SEQ, D), dtype=np.float32)
    for c in range(8):
        s, half = c // 2, c % 2
        r0 = half * QOWN
        out[s, r0:r0 + QOWN, :] = np.asarray(res.results[c]["outT"]).T
    return out


# version 34
# speedup vs baseline: 1.0821x; 1.0821x over previous
import math
from contextlib import ExitStack

import numpy as np

import concourse.bass as bass
import concourse.mybir as mybir
from concourse.bass_utils import run_bass_kernel_spmd

F32 = mybir.dt.float32
BF16 = mybir.dt.bfloat16
AF = mybir.ActivationFunctionType
ALU = mybir.AluOpType
AX = mybir.AxisListType

D = 2048
NDC = 16
SEQ = 2048
NTC = 16
QOWN = 1024
NH = 16
HD = 128
NE = 8
SCALE = 1.0 / math.sqrt(HD)
EPS = 1e-5

_ENG_SECTIONS = {"pe": "tensor", "act": "scalar", "dve": "vector",
                 "sync": "sync", "pool": "gpsimd"}


class Prog:

    def __init__(self, nc, st):
        self.nc = nc
        self.st = st
        self.items = {e: [] for e in _ENG_SECTIONS}
        self.cnt = {}
        self.sem = {}
        for e in _ENG_SECTIONS:
            self._sem(e)
        self.water = {e: {} for e in _ENG_SECTIONS}
        self.lastw = {}
        self.readers = {}
        self.targets = set()
        self.dma_sems = set()

    def _sem(self, name):
        if name not in self.sem:
            self.sem[name] = self.st.enter_context(
                self.nc.semaphore("sem_" + name))
            self.cnt[name] = 0
        return name

    def emit(self, eng, fn, reads=(), writes=(), force_event=False, dsem=None):
        isq = eng in ("sync", "pool")
        if isq and dsem == "gq":
            sname = self._sem("gq")
            isq = False
            force_event = True
        elif isq:
            sname = self._sem(dsem or ("d_" + (writes[0] if writes else eng)))
            self.dma_sems.add(sname)
        else:
            sname = eng
        waits = {}

        def need(evmap):
            for e2, c2 in (evmap or {}).items():
                if e2 == "pe" and eng == "pe":
                    continue
                if self.water[eng].get(e2, 0) >= c2:
                    continue
                waits[e2] = max(waits.get(e2, 0), c2)

        for k in reads:
            need(self.lastw.get(k))
        for k in writes:
            need(self.lastw.get(k))
            need(self.readers.get(k))
        for e2 in sorted(waits):
            c2 = waits[e2]
            self.items[eng].append(("w", e2, c2))
            self.water[eng][e2] = c2
            self.targets.add((e2, c2))
        self.cnt[sname] += 1
        c = self.cnt[sname]
        self.items[eng].append(("i", fn, sname, c))
        if force_event or isq:
            self.targets.add((sname, c))
        for k in reads:
            self.readers.setdefault(k, {})[sname] = c
        for k in writes:
            self.lastw[k] = {sname: c}
            self.readers[k] = {}
        return (sname, c)

    def barrier_wait(self, eng, ev_sem, ev_cnt):
        if self.water[eng].get(ev_sem, 0) < ev_cnt:
            self.items[eng].append(("w", ev_sem, ev_cnt))
            self.water[eng][ev_sem] = ev_cnt
            self.targets.add((ev_sem, ev_cnt))

    def finalize(self, block):
        prefix = {}
        for eng in _ENG_SECTIONS:
            if eng in ("sync", "pool"):
                continue
            n = 0
            idx = 0
            pre = [0]
            for lst_eng, items in self.items.items():
                pass
            for it in self.items[eng]:
                if it[0] == "i" and it[2] == eng:
                    idx += 1
                    if (eng, idx) in self.targets:
                        n += 1
                    pre.append(n)
            prefix[eng] = pre

        def waitval(sname, c):
            if sname in self.dma_sems:
                return 16 * c
            if sname in prefix:
                return prefix[sname][c]
            return c

        for eng, section in _ENG_SECTIONS.items():
            items = self.items[eng]
            sem = self.sem
            tgt = self.targets

            def body(E, items=items, eng=eng):
                for it in items:
                    if it[0] == "w":
                        E.wait_ge(sem[it[1]], waitval(it[1], it[2]))
                    else:
                        instr = it[1](E)
                        sname = it[2]
                        if (sname, it[3]) in tgt:
                            instr.then_inc(
                                sem[sname],
                                16 if sname in self.dma_sems else 1)

            getattr(block, section)(body)


def build_program(debug=False):
    nc = bass.Bass("TRN2", target_bir_lowering=False, debug=False)

    xtok = nc.dram_tensor("xtok", [SEQ, D], F32, kind="ExternalInput")
    xT = nc.dram_tensor("xT", [D, QOWN], F32, kind="ExternalInput")
    wq4 = nc.dram_tensor("wq4", [NH, 128, NDC, 128], F32, kind="ExternalInput")
    wk4 = nc.dram_tensor("wk4", [NH, 128, NDC, 128], F32, kind="ExternalInput")
    wv4 = nc.dram_tensor("wv4", [NH, 128, NDC, 128], F32, kind="ExternalInput")
    wo4 = nc.dram_tensor("wo4", [NDC, 128, NDC, 128], F32, kind="ExternalInput")
    wb4 = nc.dram_tensor("wb4", [NDC, 128, NDC, 128], F32, kind="ExternalInput")
    bbd = nc.dram_tensor("bbd", [128, NDC], F32, kind="ExternalInput")
    rwd = nc.dram_tensor("rwd", [128, NDC, NE], F32, kind="ExternalInput")
    atd = nc.dram_tensor("atd", [128, NDC, 128], F32, kind="ExternalInput")
    wbld = nc.dram_tensor("wbld", [128, D], F32, kind="ExternalInput")
    eb_d = nc.dram_tensor("eb_d", [NE, 128], F32, kind="ExternalInput")
    id_d = nc.dram_tensor("id_d", [128, 128], F32, kind="ExternalInput")
    cmb_d = nc.dram_tensor("cmb_d", [128, 1], F32, kind="ExternalInput")
    outT = nc.dram_tensor("outT", [D, QOWN], F32, kind="ExternalOutput")
    dbg = {}
    if debug:
        dbg["xnT"] = nc.dram_tensor("d_xnT", [128, NDC * SEQ], F32, kind="ExternalOutput")
        dbg["kt"] = nc.dram_tensor("d_kt", [128, SEQ], F32, kind="ExternalOutput")
        dbg["qt"] = nc.dram_tensor("d_qt", [128, QOWN], F32, kind="ExternalOutput")
        dbg["vh"] = nc.dram_tensor("d_vh", [128, SEQ], F32, kind="ExternalOutput")
        dbg["ct"] = nc.dram_tensor("d_ct", [128, NH * QOWN], F32, kind="ExternalOutput")
        dbg["logits"] = nc.dram_tensor("d_logits", [128, 64], F32, kind="ExternalOutput")
        dbg["hm"] = nc.dram_tensor("d_hm", [128, QOWN], F32, kind="ExternalOutput")
        dbg["mer"] = nc.dram_tensor("d_mer", [128, QOWN], F32, kind="ExternalOutput")

    with ExitStack() as st:
        def sb(name, shape, dtype):
            return st.enter_context(nc.sbuf_tensor(name, shape, dtype))

        def psum(name, shape, dtype=F32):
            return st.enter_context(nc.psum_tensor(name, shape, dtype))

        xnT = sb("xnT", [128, NDC * SEQ], BF16)
        ctall = sb("ctall", [128, NH * QOWN], BF16)
        vh = sb("vh", [128, 2 * SEQ], BF16)
        vt = sb("vt", [128, SEQ], BF16)
        kt = sb("kt", [128, 2 * SEQ], BF16)
        qt = sb("qt", [128, 2 * QOWN], BF16)
        pt = sb("pt", [128, NTC * 512], BF16)
        wqs = sb("wqs", [128, D], BF16)
        wks = sb("wks", [128, D], BF16)
        wvs = sb("wvs", [128, D], BF16)
        xst = sb("xst", [128, 2 * D], F32)
        xns = sb("xns", [128, 2 * D], BF16)
        outb = sb("outb", [128, 2 * QOWN], F32)
        ats = sb("ats", [128, D], BF16)
        wbls = sb("wbls", [128, D], BF16)
        mer = sb("mer", [128, QOWN], BF16)
        hm = sb("hm", [128, QOWN], BF16)
        maskT = sb("maskT", [8, QOWN], BF16)
        logits = sb("logits", [128, 64], F32)
        mask8 = sb("mask8", [128, 8], BF16)
        bc = sb("bc", [128, 512], F32)
        lnl = sb("lnl", [1, 512], F32)
        rlb = sb("rlb", [1, 512], BF16)
        st6 = sb("st6", [128, 48], F32)
        st2 = sb("st2", [128, 32], F32)
        sqv = sb("sqv", [128, 2], F32)
        rcol = sb("rcol", [128, 16], F32)
        epsc = sb("epsc", [128, 1], F32)
        twoc = sb("twoc", [128, 1], F32)
        mxc = sb("mxc", [128, 1], F32)
        idbf = sb("idbf", [128, 128], BF16)
        ebfs = sb("ebfs", [8, 128], BF16)
        onesc = sb("onesc", [128, 1], BF16)
        cmbs = sb("cmbs", [128, 1], BF16)
        lpart = sb("lpart", [128, 512], BF16)
        onesr = sb("onesr", [1, 128], BF16)
        bbs = sb("bbs", [128, NDC], F32)
        rws = sb("rws", [128, NDC * NE], BF16)

        psA = psum("psA", [128, 2048], F32)
        psB = psum("psB", [128, 1024], F32)
        psC = psum("psC", [128, 512], F32)
        psE = psum("psE", [128, 512], F32)

        def psa(i):
            return psA[:, i * 512:(i + 1) * 512]

        def psabf(i):
            return psA[:, i * 512:(i + 1) * 512].bitcast(BF16)

        def psbbf(i):
            return psB[:, i * 512:(i + 1) * 512].bitcast(BF16)

        p = Prog(nc, st)
        xnT3 = xnT.rearrange("p (d t) -> p d t", t=SEQ)

        def xnt(dc, lo, hi):
            return xnT[:, dc * SEQ + lo: dc * SEQ + hi]

        XNALL = [f"xnT{t}" for t in range(NTC)]

        p.emit("dve", lambda E: E.memset(psE[:], 0.0), writes=["psE"])
        p.emit("dve", lambda E: E.memset(epsc[:], EPS), writes=["epsc"])
        p.emit("dve", lambda E: E.memset(twoc[:], 2.0), writes=["twoc"])
        p.emit("dve", lambda E: E.memset(onesc[:], 1.0), writes=["onesc"])
        p.emit("dve", lambda E: E.memset(onesr[:], 1.0), writes=["onesr"])
        p.emit("pool", lambda E: E.dma_start(idbf[:], id_d[:]), writes=["idbf"])
        p.emit("pool", lambda E: E.dma_start(cmbs[:], cmb_d[:]), writes=["cmbs"])
        p.emit("pool", lambda E: E.dma_start(ebfs[:], eb_d[:]), writes=["ebfs"])
        p.emit("pool", lambda E: E.dma_start(
            rws[:], rwd[:].rearrange("p a b -> p (a b)")), writes=["rws"])
        p.emit("pool", lambda E: E.dma_start(
            ats[:], atd[:].rearrange("p a b -> p (a b)")), writes=["ats"])
        p.emit("pool", lambda E: E.dma_start(wbls[:], wbld[:]), writes=["wbls"])
        p.emit("sync", lambda E: E.dma_start(bbs[:], bbd[:]), writes=["bbs"])

        ptf = pt[:].bitcast(F32)

        xslots = [(xst[:, 0:D], "xs0", []),
                  (xst[:, D:2 * D], "xs1", []),
                  (ptf[:, 0:D], "xs2", [f"pt{i}" for i in range(4)]),
                  (ptf[:, D:2 * D], "xs3", [f"pt{i}" for i in range(4, 8)])]

        def ln_stats(tc):
            if tc >= NTC:
                return
            sl = tc % 2
            xv, xk, ptk = xslots[tc % 4]
            p.emit("sync", lambda E, tc=tc, xv=xv: E.dma_start(
                xv, xtok[tc * 128:(tc + 1) * 128, :]), writes=[xk])
            s6 = st6[:, sl * 24:(sl + 1) * 24]
            for i in range(4):
                p.emit("dve", lambda E, xv=xv, i=i, s6=s6: E.bn_stats(
                    s6[:, i * 6:(i + 1) * 6], xv[:, i * 512:(i + 1) * 512]),
                    reads=[xk], writes=[f"st6{sl}"])
            s2 = st2[:, tc * 2:(tc + 1) * 2]
            p.emit("dve", lambda E, s6=s6, s2=s2: E.bn_aggr(s2, s6),
                   reads=[f"st6{sl}"], writes=[f"st2_{tc}"])
            p.emit("act", lambda E, s2=s2, sl=sl: E.activation(
                sqv[:, sl:sl + 1], s2[:, 1:2], AF.Ln, bias=epsc[:, 0:1],
                scale=1.0), reads=[f"st2_{tc}", "epsc"], writes=[f"sqv{sl}"])
            p.emit("act", lambda E, sl=sl, tc=tc: E.activation(
                rcol[:, tc:tc + 1], sqv[:, sl:sl + 1], AF.Exp, scale=-0.5),
                reads=[f"sqv{sl}"], writes=[f"rcol{tc}"])

        def ln_norm(tc):
            sl = tc % 2
            nxv, xk, ptk = xslots[tc % 4]
            xnv = xns[:, sl * D:(sl + 1) * D]
            xnk = f"xns{sl}"
            s2 = st2[:, tc * 2:(tc + 1) * 2]
            p.emit("dve", lambda E, nxv=nxv, xnv=xnv, s2=s2, tc=tc: E.tensor_scalar(
                xnv, nxv, s2[:, 0:1], rcol[:, tc:tc + 1],
                ALU.subtract, ALU.mult),
                reads=[xk, f"st2_{tc}", f"rcol{tc}"] + ptk, writes=[xnk])
            b0, b1 = (0, 1) if tc % 2 == 0 else (2, 3)
            for half, b in ((0, b0), (1, b1)):
                for j in range(8):
                    dc = half * 8 + j
                    p.emit("pe", lambda E, b=b, j=j, dc=dc, xnv=xnv: E.transpose(
                        psabf(b)[:, j * 128:(j + 1) * 128],
                        xnv[:, dc * 128:(dc + 1) * 128], idbf[:]),
                        reads=[xnk, "idbf"], writes=[f"psA{b}"])
                p.emit("act", lambda E, b=b, half=half, tc=tc: E.copy(
                    xnT3[:, half * 8:(half + 1) * 8, tc * 128:(tc + 1) * 128],
                    psabf(b).rearrange("p (a b) -> p a b", b=128)),
                    reads=[f"psA{b}"], writes=[f"xnT{tc}"])
            rtc = tc - 1
            if 0 <= rtc < 8:
                router(rtc)
            if tc >= 8:
                mask_chain(tc - 8)

        def mask_chain(tc):
            lsl = logits[:, tc * 8:(tc + 1) * 8]
            pscbf = psC[:].bitcast(BF16)
            p.emit("dve", lambda E, lsl=lsl: E.reduce_max(mxc[:], lsl, axis=AX.X),
                   reads=[f"logits{tc}"], writes=["mxc"])
            p.emit("dve", lambda E, lsl=lsl: E.tensor_scalar(
                mask8[:], lsl, mxc[:, 0:1], None, ALU.is_ge),
                reads=[f"logits{tc}", "mxc"], writes=["mask8"])
            p.emit("pe", lambda E, pscbf=pscbf: E.transpose(
                pscbf[:, 0:128][0:8, :], mask8[:], idbf[:]),
                reads=["mask8", "idbf"], writes=["psC"])
            p.emit("act", lambda E, tc=tc, pscbf=pscbf: E.copy(
                maskT[:, tc * 128:(tc + 1) * 128], pscbf[0:8, 0:128]),
                reads=["psC"], writes=["maskT"])

        def router(tc):
            for dc in range(NDC):
                p.emit("pe", lambda E, tc=tc, dc=dc: E.matmul(
                    psE[:, 0:8], xnt(dc, tc * 128, (tc + 1) * 128),
                    rws[:, dc * 8:(dc + 1) * 8],
                    start=(dc == 0), stop=(dc == NDC - 1)),
                    reads=[f"xnT{tc}", "rws"], writes=["psE"])
            p.emit("act", lambda E, tc=tc: E.copy(
                logits[:, tc * 8:(tc + 1) * 8], psE[:, 0:8]),
                reads=["psE"], writes=[f"logits{tc}"])

        ln_stats(0)
        ln_stats(1)
        for tc in range(NTC):
            ln_norm(tc)
            ln_stats(tc + 2)

        def b2():
            for nb in range(2):
                p.emit("pe", lambda E, nb=nb: E.matmul(
                    psB[:, nb * 512:(nb + 1) * 512], ebfs[:],
                    maskT[:, nb * 512:(nb + 1) * 512], start=True, stop=True),
                    reads=["maskT", "ebfs"], writes=[f"psB{nb}"])
                p.emit("act", lambda E, nb=nb: E.copy(
                    mer[:, nb * 512:(nb + 1) * 512], psB[:, nb * 512:(nb + 1) * 512]),
                    reads=[f"psB{nb}"], writes=["mer"])
            for dc in range(NDC):
                for nb in range(2):
                    p.emit("pe", lambda E, dc=dc, nb=nb: E.matmul(
                        psA[:, 1024 + nb * 512: 1024 + (nb + 1) * 512],
                        ats[:, dc * 128:(dc + 1) * 128],
                        xnt(dc, nb * 512, (nb + 1) * 512),
                        start=(dc == 0), stop=(dc == NDC - 1)),
                        reads=["ats"] + XNALL, writes=[f"psA{2 + nb}"])
            for nb in range(2):
                p.emit("dve", lambda E, nb=nb: E.scalar_tensor_tensor(
                    hm[:, nb * 512:(nb + 1) * 512],
                    psA[:, 1024 + nb * 512:1024 + (nb + 1) * 512],
                    twoc[:, 0:1], mer[:, nb * 512:(nb + 1) * 512],
                    ALU.mult, ALU.mult),
                    reads=[f"psA{2 + nb}", "twoc", "mer"], writes=["hm"])

        def slot_bufs(h):
            sl = h % 2
            return (sl, qt[:, sl * QOWN:(sl + 1) * QOWN], f"qt{sl}",
                    kt[:, sl * SEQ:(sl + 1) * SEQ], f"kt{sl}",
                    vh[:, sl * SEQ:(sl + 1) * SEQ], f"vh{sl}")

        def wdma_qk(h):
            if h >= NH:
                return
            p.emit("pool", lambda E, h=h: E.dma_start(
                wqs[:], wq4[h].rearrange("p a b -> p (a b)")), writes=["wqs"])
            p.emit("pool", lambda E, h=h: E.dma_start(
                wks[:], wk4[h].rearrange("p a b -> p (a b)")), writes=["wks"])

        def wdma_v(h):
            if h >= NH:
                return
            p.emit("pool", lambda E, h=h: E.dma_start(
                wvs[:], wv4[h].rearrange("p a b -> p (a b)")), writes=["wvs"])

        PB = [(psB, 0, "psB0"), (psB, 512, "psB1"), (psA, 1536, "psA3")]

        def gen_proj(h):
            sl, qtv, qtk, ktv, ktk, vhv, vhk = slot_bufs(h)

            def round2(ws, wkey, banks, nbs, dst, dkey):
                for dc in range(NDC):
                    for i, nb in enumerate(nbs):
                        ps_t, col, pk = PB[banks[i]]
                        p.emit("pe", lambda E, dc=dc, nb=nb, ps_t=ps_t,
                               col=col, ws=ws: E.matmul(
                            ps_t[:, col:col + 512],
                            ws[:, dc * 128:(dc + 1) * 128],
                            xnt(dc, nb * 512, (nb + 1) * 512),
                            start=(dc == 0), stop=(dc == NDC - 1)),
                            reads=[wkey] + XNALL, writes=[pk])
                        yield
                for i, nb in enumerate(nbs):
                    ps_t, col, pk = PB[banks[i]]
                    p.emit("dve", lambda E, nb=nb, ps_t=ps_t, col=col,
                           dst=dst: E.tensor_copy(
                        dst[:, nb * 512:(nb + 1) * 512],
                        ps_t[:, col:col + 512]),
                        reads=[pk], writes=[dkey])
                    yield

            yield from round2(wqs, "wqs", [0, 1], [0, 1], qtv, qtk)
            yield from round2(wks, "wks", [2, 0], [0, 1], ktv, ktk)
            yield from round2(wks, "wks", [1, 2], [2, 3], ktv, ktk)
            wdma_qk(h + 1)
            yield from round2(wvs, "wvs", [0, 1], [0, 1], vt, "vt0")
            yield from round2(wvs, "wvs", [2, 0], [2, 3], vt, "vt1")
            wdma_v(h + 1)

        def vtrans(h):
            sl, qtv, qtk, ktv, ktk, vhv, vhk = slot_bufs(h)
            for half in range(2):
                for jj in range(8):
                    kc = half * 8 + jj
                    p.emit("pe", lambda E, half=half, jj=jj, kc=kc: E.transpose(
                        psabf(half)[:, jj * 128:(jj + 1) * 128],
                        vt[:, kc * 128:(kc + 1) * 128], idbf[:]),
                        reads=[f"vt{half}", "idbf"], writes=[f"psA{half}"])
                p.emit("act", lambda E, half=half, vhv=vhv: E.copy(
                    vhv[:, half * 1024:(half + 1) * 1024], psabf(half)),
                    reads=[f"psA{half}"], writes=[vhk])

        def attn_s_weave(h, qb, pgen, per, nbanks, blk=4):
            sl, qtv, qtk, ktv, ktk, vhv, vhk = slot_bufs(h)
            for kc0 in range(0, NTC, blk):
                for kc in range(kc0, min(kc0 + blk, NTC)):
                    bk = kc % nbanks
                    p.emit("pe", lambda E, kc=kc, qb=qb, bk=bk, ktv=ktv, qtv=qtv: E.matmul(
                        psa(bk),
                        ktv[:, kc * 128:(kc + 1) * 128],
                        qtv[:, qb * 512:(qb + 1) * 512],
                        start=True, stop=True),
                        reads=[ktk, qtk], writes=[f"psA{bk}"])
                    p.emit("act", lambda E, kc=kc, bk=bk: E.activation(
                        pt[:, kc * 512:(kc + 1) * 512], psa(bk),
                        AF.Exp, scale=SCALE),
                        reads=[f"psA{bk}"], writes=[f"pt{kc}"])
                if pgen is not None:
                    for _ in range(per * blk):
                        next(pgen, None)

        def attn_av(h, qb):
            sl, qtv, qtk, ktv, ktk, vhv, vhk = slot_bufs(h)
            for r in range(4):
                for j2 in range(4):
                    kc = 4 * r + j2
                    p.emit("pe", lambda E, kc=kc, j2=j2, r=r: E.matmul(
                        psE[32 * j2:32 * j2 + 1, :], onesc[:],
                        pt[:, kc * 512:(kc + 1) * 512],
                        start=(r == 0), stop=(r == 3),
                        tile_position=(0, 32 * j2)),
                        reads=["onesc", f"pt{kc}"], writes=["psE"])
            p.emit("act", lambda E: E.copy(lpart[:], psE[:]),
                   reads=["psE"], writes=["lpart"])
            for kc in range(12):
                p.emit("pe", lambda E, kc=kc, vhv=vhv: E.matmul(
                    psC[:], vhv[:, kc * 128:(kc + 1) * 128],
                    pt[:, kc * 512:(kc + 1) * 512],
                    start=(kc == 0), stop=False),
                    reads=[vhk, f"pt{kc}"], writes=["psC"])
            p.emit("pe", lambda E: E.matmul(
                psE[0:1, :], cmbs[:], lpart[:], start=True, stop=True),
                reads=["cmbs", "lpart"], writes=["psE"])
            p.emit("act", lambda E: E.activation(
                lnl[:], psE[0:1, :], AF.Ln),
                reads=["psE"], writes=["lnl"])
            p.emit("act", lambda E: E.activation(
                rlb[:], lnl[:], AF.Exp, scale=-1.0),
                reads=["lnl"], writes=["rlb"])
            for kc in range(12, NTC):
                p.emit("pe", lambda E, kc=kc, vhv=vhv: E.matmul(
                    psC[:], vhv[:, kc * 128:(kc + 1) * 128],
                    pt[:, kc * 512:(kc + 1) * 512],
                    start=False, stop=(kc == NTC - 1)),
                    reads=[vhk, f"pt{kc}"], writes=["psC"])
            p.emit("pe", lambda E: E.matmul(
                psE[:], onesr[:], rlb[:], start=True, stop=True),
                reads=["onesr", "rlb"], writes=["psE"])
            p.emit("act", lambda E: E.copy(bc[:], psE[:]),
                   reads=["psE"], writes=["bc"])
            p.emit("dve", lambda E, h=h, qb=qb: E.tensor_tensor(
                ctall[:, h * QOWN + qb * 512: h * QOWN + (qb + 1) * 512],
                psC[:], bc[:], ALU.mult),
                reads=["psC", "bc"], writes=["ct"])

        def dpsum(oc):
            if oc % 2 == 0:
                return psB, 0, ["psB0", "psB1"]
            return psA, 1024, ["psA2", "psA3"]

        def gen_dpre(oc):
            ps_t, off, pk = dpsum(oc)
            if oc % 2 == 0:
                wot, wok_keys = wqs, ["wqs"]
                wbv, wbk_keys = wks[:, 0:D], ["wks"]
            else:
                wot, wok_keys = wvs, ["wvs"]
                wbv, wbk_keys = vt[:, 0:D], ["vt0", "vt1"]
            dstate[oc] = (wot, wok_keys)
            p.emit("pool", lambda E, oc=oc, wot=wot: E.dma_start(
                wot[:, 0:D], wo4[oc].rearrange("p a b -> p (a b)")),
                writes=wok_keys)
            p.emit("pool", lambda E, oc=oc, wbv=wbv: E.dma_start(
                wbv, wb4[oc].rearrange("p a b -> p (a b)")), writes=wbk_keys)
            sl = oc % 2
            xv = xst[:, sl * D: sl * D + QOWN]
            p.emit("sync", lambda E, oc=oc, xv=xv: E.dma_start(
                xv, xT[oc * 128:(oc + 1) * 128, :]), writes=[f"xs{sl}"])
            yield
            for dc in range(NDC):
                for nb in range(2):
                    p.emit("pe", lambda E, dc=dc, nb=nb, ps_t=ps_t, off=off,
                           wbv=wbv, wbk_keys=wbk_keys, pk=pk: E.matmul(
                        ps_t[:, off + nb * 512: off + (nb + 1) * 512],
                        wbv[:, dc * 128:(dc + 1) * 128],
                        xnt(dc, nb * 512, (nb + 1) * 512),
                        start=(dc == 0), stop=False),
                        reads=wbk_keys + XNALL, writes=[pk[nb]])
                    yield
            for nb in range(2):
                p.emit("pe", lambda E, oc=oc, nb=nb, ps_t=ps_t, off=off, pk=pk: E.matmul(
                    ps_t[:, off + nb * 512: off + (nb + 1) * 512],
                    wbls[:, oc * 128:(oc + 1) * 128],
                    hm[:, nb * 512:(nb + 1) * 512],
                    start=False, stop=False),
                    reads=["wbls", "hm"], writes=[pk[nb]])
                yield

        def dpost(oc):
            ps_t, off, pk = dpsum(oc)
            wot, wok_keys = dstate[oc]
            sl = oc % 2
            xv = xst[:, sl * D: sl * D + QOWN]
            for dc in range(NDC):
                for nb in range(2):
                    p.emit("pe", lambda E, dc=dc, nb=nb, ps_t=ps_t, off=off, wot=wot: E.matmul(
                        ps_t[:, off + nb * 512: off + (nb + 1) * 512],
                        wot[:, dc * 128:(dc + 1) * 128],
                        ctall[:, dc * QOWN + nb * 512: dc * QOWN + (nb + 1) * 512],
                        start=False, stop=(dc == NDC - 1)),
                        reads=wok_keys + ["ct"], writes=[pk[nb]])
            p.emit("dve", lambda E, oc=oc, sl=sl, xv=xv, ps_t=ps_t, off=off, pk=pk: E.scalar_tensor_tensor(
                outb[:, sl * QOWN:(sl + 1) * QOWN],
                ps_t[:, off: off + 1024],
                bbs[:, oc:oc + 1], xv, ALU.add, ALU.add),
                reads=pk + ["bbs", f"xs{sl}"], writes=[f"outb{sl}"])
            p.emit("sync", lambda E, oc=oc, sl=sl: E.dma_start(
                outT[oc * 128:(oc + 1) * 128, :],
                outb[:, sl * QOWN:(sl + 1) * QOWN]),
                reads=[f"outb{sl}"], writes=[], dsem=f"d_out{sl}")

        dstate = {}

        wdma_qk(0)
        wdma_v(0)
        pg = gen_proj(0)
        for _ in pg:
            pass
        b2()
        vtrans(0)
        for h in range(1, NH):
            pg = gen_proj(h)
            attn_s_weave(h - 1, 0, pg, 5, 3, blk=3)
            attn_av(h - 1, 0)
            attn_s_weave(h - 1, 1, pg, 5, 3, blk=3)
            for _ in pg:
                pass
            vtrans(h)
            attn_av(h - 1, 1)
        dg0 = gen_dpre(0)
        attn_s_weave(NH - 1, 0, dg0, 2, 2, blk=2)
        for _ in dg0:
            pass
        attn_av(NH - 1, 0)
        dg1 = gen_dpre(1)
        attn_s_weave(NH - 1, 1, dg1, 2, 2, blk=2)
        for _ in dg1:
            pass
        attn_av(NH - 1, 1)
        dpost(0)
        dpost(1)
        for oc in range(2, NDC):
            for _ in gen_dpre(oc):
                pass
            dpost(oc)

        if debug:
            for name, src, reads in [
                ("xnT", xnT[:], XNALL), ("kt", kt[:, SEQ:2*SEQ], ["kt1"]),
                ("qt", qt[:, QOWN:2*QOWN], ["qt1"]), ("vh", vh[:, SEQ:2*SEQ], ["vh1"]),
                ("ct", ctall[:], ["ct"]), ("logits", logits[:], [f"logits{t}" for t in range(8)]),
                ("hm", hm[:], ["hm"]), ("mer", mer[:], ["mer"]),
            ]:
                p.emit("pool", lambda E, name=name, src=src: E.dma_start(
                    dbg[name][:], src), reads=reads, writes=["dbg_" + name],
                    dsem="d_dbg")
            p.barrier_wait("sync", "d_dbg", p.cnt.get("d_dbg", 0))

        p.barrier_wait("sync", "d_out0", p.cnt.get("d_out0", 0))
        p.barrier_wait("sync", "d_out1", p.cnt.get("d_out1", 0))

        block = st.enter_context(nc.Block())
        p.finalize(block)
    return nc


_NC_CACHE = {}


def _get_nc(debug=False):
    if debug not in _NC_CACHE:
        _NC_CACHE[debug] = build_program(debug)
    return _NC_CACHE[debug]


def cmb_col():
    c = np.zeros((128, 1), np.float32)
    c[[0, 32, 64, 96], 0] = 1.0
    return c


def _prep_shared(Wq, Wk, Wv, Wo, Wb, bb, rW, A, Bw):
    def pe4(w):
        return np.ascontiguousarray(
            w.reshape(16, 128, 16, 128).transpose(0, 3, 2, 1))

    d = {
        "wq4": pe4(Wq), "wk4": pe4(Wk), "wv4": pe4(Wv),
        "wo4": pe4(Wo), "wb4": pe4(Wb),
        "bbd": np.ascontiguousarray(bb.reshape(16, 128).T),
        "rwd": np.ascontiguousarray(rW.reshape(8, 16, 128).transpose(2, 1, 0)),
        "atd": np.ascontiguousarray(
            A.reshape(128, 2048).reshape(128, 16, 128).transpose(2, 1, 0)),
        "wbld": np.ascontiguousarray(Bw.transpose(0, 2, 1).reshape(128, 2048)),
        "eb_d": np.repeat(np.eye(8, dtype=np.float32), 16, axis=1),
        "id_d": np.eye(128, dtype=np.float32),
        "cmb_d": cmb_col(),
    }
    return d


def kernel(L0_attn_hidden_states, L0_attn_Wq, L0_attn_Wk, L0_attn_Wv,
           L0_attn_Wo, L0_ffn_W_base, L0_ffn_b_base, L0_ffn_router_W,
           L0_ffn_A, L0_ffn_Bw):
    x = np.ascontiguousarray(np.asarray(L0_attn_hidden_states, dtype=np.float32))
    shared = _prep_shared(
        np.asarray(L0_attn_Wq, np.float32), np.asarray(L0_attn_Wk, np.float32),
        np.asarray(L0_attn_Wv, np.float32), np.asarray(L0_attn_Wo, np.float32),
        np.asarray(L0_ffn_W_base, np.float32),
        np.asarray(L0_ffn_b_base, np.float32),
        np.asarray(L0_ffn_router_W, np.float32),
        np.asarray(L0_ffn_A, np.float32), np.asarray(L0_ffn_Bw, np.float32))

    in_maps = []
    for c in range(8):
        s, half = c // 2, c % 2
        r0 = half * QOWN
        own = x[s, r0:r0 + QOWN]
        other = x[s, QOWN - r0:SEQ - r0] if half == 0 else x[s, 0:QOWN]
        m = dict(shared)
        m["xtok"] = np.ascontiguousarray(np.concatenate([own, other], axis=0))
        m["xT"] = np.ascontiguousarray(own.T)
        in_maps.append(m)

    nc = _get_nc(False)
    res = run_bass_kernel_spmd(nc, in_maps, core_ids=list(range(8)))
    out = np.empty((4, SEQ, D), dtype=np.float32)
    for c in range(8):
        s, half = c // 2, c % 2
        r0 = half * QOWN
        out[s, r0:r0 + QOWN, :] = np.asarray(res.results[c]["outT"]).T
    return out
